# version 02
# baseline (speedup 1.0000x reference)
import math
from contextlib import ExitStack

import numpy as np
import ml_dtypes
import concourse.bass as bass
import concourse.mybir as mybir
from concourse.bass_utils import run_bass_kernel_spmd

F32 = mybir.dt.float32
BF16 = mybir.dt.bfloat16
AF = mybir.ActivationFunctionType
ALU = mybir.AluOpType
AX = mybir.AxisListType
BF = ml_dtypes.bfloat16

NCORES = 8
D = 1024
B = 4
T = 4096
NTOK = B * T
TPC = NTOK // NCORES


def _tname(t):
    if isinstance(t, bass.AP):
        return t.tensor.name
    return t.name


class Prog:
    COMPUTE = ("vector", "scalar", "tensor", "gpsimd")

    def __init__(self, name):
        self.name = name
        self.nc = bass.Bass("TRN2", target_bir_lowering=False, detect_race_conditions=False)
        self.es = ExitStack()
        self.ops = []
        self.reads = []
        self.writes = []
        self._uid = 0

    def inp(self, name, shape, dtype=F32):
        return self.nc.dram_tensor(name, list(shape), dtype, kind="ExternalInput").ap()

    def out(self, name, shape, dtype=F32):
        return self.nc.dram_tensor(name, list(shape), dtype, kind="ExternalOutput").ap()

    def sbuf(self, shape, dtype):
        self._uid += 1
        return self.es.enter_context(
            self.nc.sbuf_tensor(f"sb{self._uid}", list(shape), dtype))

    def psum(self, shape, dtype=F32):
        self._uid += 1
        return self.es.enter_context(
            self.nc.psum_tensor(f"ps{self._uid}", list(shape), dtype))

    def op(self, engine, emit, reads=(), writes=()):
        self.ops.append((engine, "c", emit))
        self.reads.append(tuple(_tname(t) for t in reads))
        self.writes.append(tuple(_tname(t) for t in writes))

    def dma(self, engine, out_ap, in_ap):
        def emit(eng, out_ap=out_ap, in_ap=in_ap):
            return eng.dma_start(out=out_ap, in_=in_ap)
        self.ops.append((engine, "d", emit))
        self.reads.append((_tname(in_ap),))
        self.writes.append((_tname(out_ap),))

    def build(self):
        nc = self.nc
        n = len(self.ops)
        semkeys = sorted({(e, k) for (e, k, _) in self.ops})
        sems = {}
        for (e, k) in semkeys:
            sems[(e, k)] = self.es.enter_context(nc.semaphore(f"sem_{e}_{k}"))
        seq = {}
        counters = {k: 0 for k in semkeys}
        for i, (e, k, _) in enumerate(self.ops):
            seq[i] = counters[(e, k)]
            counters[(e, k)] += 1
        totals = dict(counters)

        def done_val(i):
            e, k, _ = self.ops[i]
            stepv = 16 if k == "d" else 1
            return sems[(e, k)], stepv * (seq[i] + 1)

        last_writer = {}
        readers_since = {}
        deps = [set() for _ in range(n)]
        for i in range(n):
            for t in self.reads[i]:
                if t in last_writer:
                    deps[i].add(last_writer[t])
            for t in self.writes[i]:
                if t in last_writer:
                    deps[i].add(last_writer[t])
                for r in readers_since.get(t, ()):
                    deps[i].add(r)
            for t in self.reads[i]:
                readers_since.setdefault(t, []).append(i)
            for t in self.writes[i]:
                last_writer[t] = i
                readers_since[t] = []

        waits = [[] for _ in range(n)]
        eng_hi = {}
        for i in range(n):
            e, k, _ = self.ops[i]
            cur = eng_hi.setdefault(e, {})
            if k == "d" and seq[i] > 0:
                sem0, val0 = sems[(e, "d")], 16 * seq[i]
                if cur.get(sem0, -1) < val0:
                    waits[i].append((sem0, val0))
                    cur[sem0] = val0
            for d_ in sorted(deps[i]):
                sem, val = done_val(d_)
                if cur.get(sem, -1) >= val:
                    continue
                waits[i].append((sem, val))
                cur[sem] = val

        with nc.Block() as block:
            for (ename, kinds) in _group_engines(self.ops):
                deco = getattr(block, ename)

                def emit_stream(eng, ename=ename):
                    for i, (e, k, fn) in enumerate(self.ops):
                        if e != ename:
                            continue
                        for (sem, val) in waits[i]:
                            eng.wait_ge(sem, val)
                        last = fn(eng)
                        sem, val = done_val(i)
                        step = 16 if k == "d" else 1
                        last.then_inc(sem, step)
                    if (ename, "d") in totals and totals[(ename, "d")]:
                        eng.wait_ge(sems[(ename, "d")], 16 * totals[(ename, "d")])

                deco(emit_stream)
        self.es.close()
        return nc


def _group_engines(ops):
    seen = []
    for (e, k, _) in ops:
        if e not in [x[0] for x in seen]:
            seen.append((e, [k]))
    return seen


def add_identity(p, dtype=BF16):
    ident = p.sbuf([128, 128], dtype)

    def emit_ms(eng):
        return eng.memset(ident[:], 0.0)

    def emit_sel(eng):
        return eng.affine_select(out=ident[:], in_=ident[:], compare_op=ALU.not_equal,
                                 fill=1.0, base=0, pattern=[[-1, 128]], channel_multiplier=1)
    p.op("gpsimd", emit_ms, writes=[ident[:]])
    p.op("gpsimd", emit_sel, reads=[ident[:]], writes=[ident[:]])
    return ident


def cdiv(a, b):
    return (a + b - 1) // b


def emit_tm_to_fm(p, src_hbm, dst_fm, n_rows, width_feat, ident, col0=0,
                  ld_eng="sync", pt=None):
    nt = cdiv(n_rows, 128)
    nch = width_feat // 128
    ld = [p.sbuf([128, width_feat], F32) for _ in range(2)]
    cv = [p.sbuf([128, width_feat], BF16) for _ in range(2)]
    if pt is None:
        pt = [p.psum([128, 128], BF16) for _ in range(2)]
    for i in range(nt):
        r0 = i * 128
        r1 = min(n_rows, r0 + 128)
        nr = r1 - r0
        bi = i % 2
        p.dma(ld_eng, ld[bi][0:nr, :], src_hbm[r0:r1, :])

        def emit_cv(eng, bi=bi, nr=nr):
            return eng.activation(out=cv[bi][0:nr, :], in_=ld[bi][0:nr, :], func=AF.Copy)
        p.op("scalar", emit_cv, reads=[ld[bi][:]], writes=[cv[bi][:]])
        for j in range(nch):
            pj = pt[(i * nch + j) % len(pt)]

            def emit_tr(eng, bi=bi, j=j, pj=pj, nr=nr):
                return eng.transpose(out=pj[0:128, 0:nr], in_=cv[bi][0:nr, j * 128:(j + 1) * 128],
                                     identity=ident[0:nr, 0:nr])
            p.op("tensor", emit_tr, reads=[cv[bi][:], ident[:]], writes=[pj[:]])

            def emit_ev(eng, pj=pj, j=j, c0=col0 + r0, nr=nr):
                return eng.tensor_copy(out=dst_fm[:, j, c0:c0 + nr], in_=pj[0:128, 0:nr])
            p.op("vector", emit_ev, reads=[pj[:]], writes=[dst_fm[:]])



def emit_tm_to_fm2(p, src_hbm, dst_h, dst_l, n_rows, width_feat, ident, pt=None, bufs=None):
    nt = cdiv(n_rows, 128)
    nch = width_feat // 128
    if bufs is None:
        ld = [p.sbuf([128, width_feat], F32) for _ in range(2)]
        cv = [p.sbuf([128, width_feat], BF16) for _ in range(2)]
        cl = [p.sbuf([128, width_feat], BF16) for _ in range(2)]
        cvf = p.sbuf([128, width_feat], F32)
    else:
        ld, cv, cl, cvf = bufs
    if pt is None:
        pt = [p.psum([128, 128], BF16) for _ in range(2)]
    for i in range(nt):
        r0 = i * 128
        r1 = min(n_rows, r0 + 128)
        nr = r1 - r0
        bi = i % 2
        p.dma("sync", ld[bi][0:nr, :], src_hbm[r0:r1, :])

        def emit_cv(eng, bi=bi, nr=nr):
            return eng.activation(out=cv[bi][0:nr, :], in_=ld[bi][0:nr, :], func=AF.Copy)
        p.op("scalar", emit_cv, reads=[ld[bi][:]], writes=[cv[bi][:]])

        def emit_cvf(eng, bi=bi, nr=nr):
            return eng.tensor_copy(out=cvf[0:nr, :], in_=cv[bi][0:nr, :])
        p.op("vector", emit_cvf, reads=[cv[bi][:]], writes=[cvf[:]])

        def emit_sub(eng, bi=bi, nr=nr):
            return eng.tensor_sub(out=cvf[0:nr, :], in0=ld[bi][0:nr, :], in1=cvf[0:nr, :])
        p.op("vector", emit_sub, reads=[ld[bi][:], cvf[:]], writes=[cvf[:]])

        def emit_cl(eng, bi=bi, nr=nr):
            return eng.activation(out=cl[bi][0:nr, :], in_=cvf[0:nr, :], func=AF.Copy)
        p.op("scalar", emit_cl, reads=[cvf[:]], writes=[cl[bi][:]])
        for (cvt, dstt) in ((cv, dst_h), (cl, dst_l)):
            for j in range(nch):
                pj = pt[(i * nch + j) % len(pt)]

                def emit_tr(eng, bi=bi, j=j, pj=pj, nr=nr, cvt=cvt):
                    return eng.transpose(out=pj[0:128, 0:nr], in_=cvt[bi][0:nr, j * 128:(j + 1) * 128],
                                         identity=ident[0:nr, 0:nr])
                p.op("tensor", emit_tr, reads=[cvt[bi][:], ident[:]], writes=[pj[:]])

                def emit_ev(eng, pj=pj, j=j, c0=r0, nr=nr, dstt=dstt):
                    return eng.tensor_copy(out=dstt[:, j, c0:c0 + nr], in_=pj[0:128, 0:nr])
                p.op("vector", emit_ev, reads=[pj[:]], writes=[dstt[:]])


def build_p1():
    p = Prog("p1")
    nc = p.nc
    NH = 2176
    NU = 2052
    x_pad = p.inp("x_pad", [NH, D])
    w_in = p.inp("w_in", [2 * D, D])
    convw = p.inp("convw", [128, 8, 5])
    convb = p.inp("convb", [128, 8])
    gvT = p.out("gvT", [D, TPC])

    ident = add_identity(p)
    w_fm = p.sbuf([128, 8, 2 * D], BF16)
    w_fl = p.sbuf([128, 8, 2 * D], BF16)
    bufs2 = ([p.sbuf([128, D], F32) for _ in range(2)],
             [p.sbuf([128, D], BF16) for _ in range(2)],
             [p.sbuf([128, D], BF16) for _ in range(2)],
             p.sbuf([128, D], F32))
    emit_tm_to_fm2(p, w_in, w_fm, w_fl, 2 * D, D, ident, bufs=bufs2)
    x_fm = p.sbuf([128, 8, NH], BF16)
    x_fl = p.sbuf([128, 8, NH], BF16)
    emit_tm_to_fm2(p, x_pad, x_fm, x_fl, NH, D, ident, bufs=bufs2)

    cw = p.sbuf([128, 8, 5], F32)
    cb = p.sbuf([128, 8], F32)
    p.dma("sync", cw[:], convw)
    p.dma("sync", cb[:], convb)

    tts = [(0, 512), (512, 512), (1024, 512), (1536, 512), (2048, 4)]
    pg = p.psum([128, 512])
    pv = p.psum([128, 512])
    g_sb = p.sbuf([128, NU], F32)
    val_sb = p.sbuf([128, NU], F32)
    acc = p.sbuf([128, TPC], F32)
    gv_sb = p.sbuf([128, TPC], F32)
    for j in range(8):
        for (c0, w) in tts:
            def emit_mm(eng, j=j, c0=c0, w=w):
                pairs = [(w_fm, x_fm), (w_fm, x_fl), (w_fl, x_fm)]
                for pi_, (wt, xt) in enumerate(pairs):
                    for dc in range(8):
                        last = eng.matmul(out=pg[:, 0:w], lhsT=wt[:, dc, j * 128:(j + 1) * 128],
                                          rhs=xt[:, dc, c0:c0 + w],
                                          start=(pi_ == 0 and dc == 0), stop=(pi_ == 2 and dc == 7))
                return last
            p.op("tensor", emit_mm, reads=[w_fm[:], x_fm[:], w_fl[:], x_fl[:]], writes=[pg[:]])

            def emit_mmv(eng, j=j, c0=c0, w=w):
                pairs = [(w_fm, x_fm), (w_fm, x_fl), (w_fl, x_fm)]
                for pi_, (wt, xt) in enumerate(pairs):
                    for dc in range(8):
                        last = eng.matmul(out=pv[:, 0:w], lhsT=wt[:, dc, (8 + j) * 128:(9 + j) * 128],
                                          rhs=xt[:, dc, c0:c0 + w],
                                          start=(pi_ == 0 and dc == 0), stop=(pi_ == 2 and dc == 7))
                return last
            p.op("tensor", emit_mmv, reads=[w_fm[:], x_fm[:], w_fl[:], x_fl[:]], writes=[pv[:]])

            def emit_sig(eng, c0=c0, w=w):
                return eng.activation(out=g_sb[:, c0:c0 + w], in_=pg[:, 0:w], func=AF.Sigmoid)
            p.op("scalar", emit_sig, reads=[pg[:]], writes=[g_sb[:]])

            def emit_ev(eng, c0=c0, w=w):
                return eng.tensor_copy(out=val_sb[:, c0:c0 + w], in_=pv[:, 0:w])
            p.op("vector", emit_ev, reads=[pv[:]], writes=[val_sb[:]])

        def emit_tap0(eng, j=j):
            return eng.activation(out=acc[:], in_=val_sb[:, 0:TPC], func=AF.Copy,
                                  scale=cw[:, j, 0:1])
        p.op("scalar", emit_tap0, reads=[val_sb[:], cw[:]], writes=[acc[:]])
        for k in range(1, 5):
            def emit_tap(eng, j=j, k=k):
                return eng.scalar_tensor_tensor(out=acc[:], in0=val_sb[:, k:k + TPC],
                                                scalar=cw[:, j, k:k + 1], in1=acc[:],
                                                op0=ALU.mult, op1=ALU.add)
            p.op("vector", emit_tap, reads=[val_sb[:], cw[:], acc[:]], writes=[acc[:]])

        def emit_bias(eng, j=j):
            return eng.scalar_tensor_tensor(out=acc[:], in0=acc[:], scalar=cb[:, j:j + 1],
                                            in1=acc[:], op0=ALU.add, op1=ALU.bypass)
        p.op("vector", emit_bias, reads=[acc[:], cb[:]], writes=[acc[:]])

        def emit_gv(eng, j=j):
            return eng.tensor_mul(out=gv_sb[:], in0=g_sb[:, 2:2 + TPC], in1=acc[:])
        p.op("vector", emit_gv, reads=[g_sb[:], acc[:]], writes=[gv_sb[:]])
        p.dma("gpsimd", gvT[j * 128:(j + 1) * 128, :], gv_sb[:])
    return p.build()


def build_p2():
    p = Prog("p2")
    gv = p.inp("gv", [512, T])
    apar = p.inp("apar", [128, 4])
    yT = p.out("yT", [512, T])

    sa = p.sbuf([128, 4], F32)
    ones = p.sbuf([128, T], F32)
    p.dma("sync", sa[:], apar)

    def emit_tanh(eng):
        return eng.activation(out=sa[:], in_=sa[:], func=AF.Tanh)
    p.op("scalar", emit_tanh, reads=[sa[:]], writes=[sa[:]])

    def emit_ones(eng):
        return eng.memset(ones[:], 1.0)
    p.op("vector", emit_ones, writes=[ones[:]])

    sgv2 = [p.sbuf([128, T], F32) for _ in range(2)]
    decb2 = [p.sbuf([128, T], F32) for _ in range(2)]
    sy2 = [p.sbuf([128, T], F32) for _ in range(2)]
    for dc in range(4):
        sgv, decb, sy = sgv2[dc % 2], decb2[dc % 2], sy2[dc % 2]
        p.dma("sync", sgv[:], gv[dc * 128:(dc + 1) * 128, :])

        def emit_decb(eng, dc=dc, decb=decb):
            return eng.activation(out=decb[:], in_=ones[:], func=AF.Copy,
                                  scale=sa[:, dc:dc + 1])
        p.op("scalar", emit_decb, reads=[ones[:], sa[:]], writes=[decb[:]])

        def emit_scan(eng, sgv=sgv, decb=decb, sy=sy):
            return eng.tensor_tensor_scan(out=sy[:], data0=decb[:], data1=sgv[:],
                                          initial=0.0, op0=ALU.mult, op1=ALU.add)
        p.op("vector", emit_scan, reads=[decb[:], sgv[:]], writes=[sy[:]])
        p.dma("gpsimd", yT[dc * 128:(dc + 1) * 128, :], sy[:])
    return p.build()


def emit_layernorm(p, src, dst, wb, scratch, eps=1e-5, nr=128):
    red, mu1, sd, epst = scratch
    def e_rs(eng):
        return eng.reduce_sum(out=red[0:nr, :], in_=src[0:nr, :], axis=AX.X)
    p.op("vector", e_rs, reads=[src[:]], writes=[red[:]])

    def e_negmu(eng):
        return eng.activation(out=mu1[0:nr, :], in_=red[0:nr, :], func=AF.Copy, scale=-1.0 / D)
    p.op("scalar", e_negmu, reads=[red[:]], writes=[mu1[:]])

    def e_center(eng):
        return eng.scalar_tensor_tensor(out=dst[0:nr, :], in0=src[0:nr, :],
                                        scalar=mu1[0:nr, 0:1], in1=src[0:nr, :],
                                        op0=ALU.add, op1=ALU.bypass)
    p.op("vector", e_center, reads=[src[:], mu1[:]], writes=[dst[:]])

    def e_sq(eng):
        return eng.activation(out=sd[0:nr, :], in_=dst[0:nr, :], func=AF.Square)
    p.op("scalar", e_sq, reads=[dst[:]], writes=[sd[:]])

    def e_rs2(eng):
        return eng.reduce_sum(out=red[0:nr, :], in_=sd[0:nr, :], axis=AX.X)
    p.op("vector", e_rs2, reads=[sd[:]], writes=[red[:]])

    def e_sqrt(eng):
        return eng.activation(out=red[0:nr, :], in_=red[0:nr, :], func=AF.Sqrt,
                              scale=1.0 / D, bias=epst[0:nr, 0:1])
    p.op("scalar", e_sqrt, reads=[red[:], epst[:]], writes=[red[:]])

    def e_recip(eng):
        return eng.reciprocal(out=red[0:nr, :], in_=red[0:nr, :])
    p.op("vector", e_recip, reads=[red[:]], writes=[red[:]])

    def e_scale(eng):
        return eng.activation(out=dst[0:nr, :], in_=dst[0:nr, :], func=AF.Copy, scale=red[0:nr, 0:1])
    p.op("scalar", e_scale, reads=[dst[:], red[:]], writes=[dst[:]])

    if wb is not None:
        wt, bt = wb

        def e_w(eng):
            return eng.tensor_mul(out=dst[0:nr, :], in0=dst[0:nr, :], in1=wt[0:nr, :])
        p.op("vector", e_w, reads=[dst[:], wt[:]], writes=[dst[:]])

        def e_b(eng):
            return eng.tensor_add(out=dst[0:nr, :], in0=dst[0:nr, :], in1=bt[0:nr, :])
        p.op("vector", e_b, reads=[dst[:], bt[:]], writes=[dst[:]])


def emit_row_bcast(p, row_hbm, dst, ones_col, n, ps=None):
    r = p.sbuf([1, n], F32)
    p.dma("sync", r[:], row_hbm)
    nb = cdiv(n, 512)
    if ps is None:
        ps = [p.psum([128, 512]) for _ in range(min(nb, 2))]
    for i in range(nb):
        c0 = i * 512
        w = min(512, n - c0)
        pp = ps[i % len(ps)]

        def e_mm(eng, c0=c0, w=w, pp=pp):
            return eng.matmul(out=pp[:, 0:w], lhsT=ones_col[:], rhs=r[:, c0:c0 + w],
                              start=True, stop=True)
        p.op("tensor", e_mm, reads=[r[:], ones_col[:]], writes=[pp[:]])

        def e_ev(eng, c0=c0, w=w, pp=pp):
            return eng.tensor_copy(out=dst[:, c0:c0 + w], in_=pp[:, 0:w])
        p.op("vector", e_ev, reads=[pp[:]], writes=[dst[:]])


def add_ones_col(p):
    ones_col = p.sbuf([1, 128], F32)

    def e(eng):
        return eng.memset(ones_col[:], 1.0)
    p.op("gpsimd", e, writes=[ones_col[:]])
    return ones_col


def build_p3():
    p = Prog("p3")
    yT_in = p.inp("yT", [D, TPC])
    x_tm = p.inp("x_tm", [TPC, D])
    w_out = p.inp("w_out", [D, D])
    lnw = p.inp("lnw", [1, D])
    lnb = p.inp("lnb", [1, D])
    x1 = p.out("x1", [TPC, D])

    ident = add_identity(p)
    ones_col = add_ones_col(p)
    wo_fm = p.sbuf([128, 8, D], BF16)
    emit_tm_to_fm(p, w_out, wo_fm, D, D, ident)
    y_b = p.sbuf([128, 8, TPC], BF16)
    yld = [p.sbuf([128, TPC], F32) for _ in range(2)]
    for dc in range(8):
        bi = dc % 2
        p.dma("sync", yld[bi][:], yT_in[dc * 128:(dc + 1) * 128, :])

        def e_cv(eng, dc=dc, bi=bi):
            return eng.activation(out=y_b[:, dc, :], in_=yld[bi][:], func=AF.Copy)
        p.op("scalar", e_cv, reads=[yld[bi][:]], writes=[y_b[:]])

    lnw_b = p.sbuf([128, D], F32)
    lnb_b = p.sbuf([128, D], F32)
    emit_row_bcast(p, lnw, lnw_b, ones_col, D)
    emit_row_bcast(p, lnb, lnb_b, ones_col, D)

    xb = [p.sbuf([128, D], F32) for _ in range(2)]
    ps = [p.psum([128, 512]) for _ in range(2)]
    res = [p.sbuf([128, D], F32) for _ in range(2)]
    t2 = [p.sbuf([128, D], F32) for _ in range(2)]
    epst = p.sbuf([128, 1], F32)

    def e_eps(eng):
        return eng.memset(epst[:], 1e-5)
    p.op("gpsimd", e_eps, writes=[epst[:]])
    scratch = (p.sbuf([128, 1], F32), p.sbuf([128, 1], F32), p.sbuf([128, D], F32), epst)
    for q in range(TPC // 128):
        bi = q % 2
        p.dma("sync", xb[bi][:], x_tm[q * 128:(q + 1) * 128, :])
        for h in range(2):
            def e_mm(eng, q=q, h=h, bi=bi):
                for dc in range(8):
                    last = eng.matmul(out=ps[h][:], lhsT=y_b[:, dc, q * 128:(q + 1) * 128],
                                      rhs=wo_fm[:, dc, h * 512:(h + 1) * 512],
                                      start=(dc == 0), stop=(dc == 7))
                return last
            p.op("tensor", e_mm, reads=[y_b[:], wo_fm[:]], writes=[ps[h][:]])

            def e_add(eng, q=q, h=h, bi=bi):
                return eng.tensor_add(out=res[bi][:, h * 512:(h + 1) * 512],
                                      in0=ps[h][:], in1=xb[bi][:, h * 512:(h + 1) * 512])
            p.op("vector", e_add, reads=[ps[h][:], xb[bi][:]], writes=[res[bi][:]])
        emit_layernorm(p, res[bi], t2[bi], (lnw_b, lnb_b), scratch)

        def e_r2(eng, bi=bi):
            return eng.tensor_add(out=t2[bi][:], in0=t2[bi][:], in1=xb[bi][:])
        p.op("vector", e_r2, reads=[t2[bi][:], xb[bi][:]], writes=[t2[bi][:]])
        emit_layernorm(p, t2[bi], t2[bi], None, scratch)
        p.dma("gpsimd", x1[q * 128:(q + 1) * 128, :], t2[bi][:])
    return p.build()


def pad_rows(x2d, r0, r1):
    n = r1 - r0
    out = np.zeros((n, x2d.shape[1]), dtype=x2d.dtype)
    s0 = max(r0, 0)
    s1 = min(r1, x2d.shape[0])
    if s1 > s0:
        out[s0 - r0:s1 - r0] = x2d[s0:s1]
    return out


def col_layout(v, nch):
    n = v.shape[0]
    out = np.zeros((128, nch), dtype=np.float32)
    for j in range(nch):
        seg = v[j * 128:min(n, (j + 1) * 128)]
        out[:len(seg), j] = seg
    return out


_prog_cache = {}


def get_prog(name, builder):
    if name not in _prog_cache:
        _prog_cache[name] = builder()
    return _prog_cache[name]


def run(name, builder, in_maps):
    nc = get_prog(name, builder)
    res = run_bass_kernel_spmd(nc, in_maps, core_ids=list(range(NCORES)))
    return res.results


def l0_attn(x, W_in, conv_w, conv_b, a_param, W_out, ln_w, ln_b):
    x_tm = np.ascontiguousarray(x.reshape(NTOK, D))
    in_maps = []
    convw_l = np.ascontiguousarray(
        conv_w[:, 0, :].reshape(8, 128, 5).transpose(1, 0, 2)).astype(np.float32)
    convb_l = col_layout(conv_b, 8)
    for c in range(NCORES):
        t0 = c * TPC
        b = t0 // T
        lo = max(t0 - 2, b * T)
        hi = min(t0 + TPC + 2, (b + 1) * T)
        xp = np.zeros((2176, D), dtype=np.float32)
        xp[lo - (t0 - 2):hi - (t0 - 2)] = x_tm[lo:hi]
        in_maps.append({"x_pad": xp, "w_in": W_in, "convw": convw_l, "convb": convb_l})
    r1 = run("p1", build_p1, in_maps)
    gvT = np.concatenate([r["gvT"] for r in r1], axis=1)

    in_maps = []
    for c in range(NCORES):
        b, dh = c // 2, c % 2
        in_maps.append({
            "gv": np.ascontiguousarray(gvT[dh * 512:(dh + 1) * 512, b * T:(b + 1) * T]),
            "apar": col_layout(a_param[dh * 512:(dh + 1) * 512], 4),
        })
    r2 = run("p2", build_p2, in_maps)
    yT = np.zeros((D, NTOK), dtype=np.float32)
    for c in range(NCORES):
        b, dh = c // 2, c % 2
        yT[dh * 512:(dh + 1) * 512, b * T:(b + 1) * T] = r2[c]["yT"]

    in_maps = []
    for c in range(NCORES):
        t0 = c * TPC
        in_maps.append({
            "yT": np.ascontiguousarray(yT[:, t0:t0 + TPC]),
            "x_tm": x_tm[t0:t0 + TPC],
            "w_out": W_out,
            "lnw": ln_w.reshape(1, D),
            "lnb": ln_b.reshape(1, D),
        })
    r3 = run("p3", build_p3, in_maps)
    return np.concatenate([r["x1"] for r in r3], axis=0)


def kernel(**inputs):
    ii = {k: np.ascontiguousarray(np.asarray(v)) for k, v in inputs.items()}
    x1 = l0_attn(ii["L0_attn_x"], ii["L0_attn_W_in"], ii["L0_attn_conv_w"],
                 ii["L0_attn_conv_b"], ii["L0_attn_a_param"], ii["L0_attn_W_out"],
                 ii["L0_attn_ln_w"], ii["L0_attn_ln_b"])
    x2 = moe(x1, ii["L0_ffn_router_W"], ii["L0_ffn_router_b"], ii["L0_ffn_W1"],
             ii["L0_ffn_b1"], ii["L0_ffn_W2"], ii["L0_ffn_b2"])
    x3 = l1_attn(x2, ii["L1_attn_Wq"], ii["L1_attn_Wk"], ii["L1_attn_Wv"],
                 ii["L1_attn_Wo"])
    xl = x3
    for i in range(6):
        xl = l1_ffn_layer_attn(xl, i, ii["L1_ffn_Wqkv"], ii["L1_ffn_q_scale"],
                               ii["L1_ffn_k_scale"], ii["L1_ffn_Wo"])
        xl = l1_ffn_layer_ffn(xl, i, ii["L1_ffn_ln_g"], ii["L1_ffn_ln_b"],
                              ii["L1_ffn_W1"], ii["L1_ffn_b1"], ii["L1_ffn_W2"],
                              ii["L1_ffn_b2"])
    out = final_ln_sum(x3, xl)
    return out.reshape(B, T, D).astype(np.float32)


def build_p4():
    p = Prog("p4")
    x1_tm = p.inp("x1_tm", [TPC, D])
    rW = p.inp("rW", [D, 8])
    rb = p.inp("rb", [1, 8])
    W1 = p.inp("W1", [8, D, 4096])
    b1h = p.inp("b1h", [128, 32, 8])
    W2 = p.inp("W2", [8, 4096, D])
    b2 = p.inp("b2", [8, D])
    x2 = p.out("x2", [TPC, D])

    ident = add_identity(p)
    ones_col = add_ones_col(p)
    NQ = TPC // 128
    psA = p.psum([128, 512])
    psB = [p.psum([128, 128], BF16)]
    psBf = p.psum([128, 128], F32)

    x1_fm = p.sbuf([128, 8, TPC], BF16)
    emit_tm_to_fm(p, x1_tm, x1_fm, TPC, D, ident, pt=psB)

    rw_f = p.sbuf([128, 8, 8], F32)
    rw_b = p.sbuf([128, 8, 8], BF16)
    for dc in range(8):
        p.dma("sync", rw_f[:, dc, :], rW[dc * 128:(dc + 1) * 128, :])

    def e_rwb(eng):
        return eng.activation(out=rw_b[:], in_=rw_f[:], func=AF.Copy)
    p.op("scalar", e_rwb, reads=[rw_f[:]], writes=[rw_b[:]])

    rb_b = p.sbuf([128, 8], F32)
    emit_row_bcast(p, rb, rb_b, ones_col, 8, ps=[psA])

    b2_f = p.sbuf([8, D], F32)
    b2_b = p.sbuf([8, D], BF16)
    p.dma("sync", b2_f[:], b2)

    def e_b2b(eng):
        return eng.activation(out=b2_b[:], in_=b2_f[:], func=AF.Copy)
    p.op("scalar", e_b2b, reads=[b2_f[:]], writes=[b2_b[:]])

    b1_s = p.sbuf([128, 32, 8], F32)
    p.dma("sync", b1_s[:], b1h)

    gates = p.sbuf([128, NQ, 8], F32)
    gatesT = p.sbuf([8, NQ, 128], F32)
    ps_r = p.psum([128, 8])
    ps_gt = psBf[0:8, :]
    rred = p.sbuf([128, 1], F32)
    rneg = p.sbuf([128, 1], F32)
    rsum = p.sbuf([128, 1], F32)
    riseq = p.sbuf([128, 8], F32)
    identf = add_identity(p, F32)
    for q in range(NQ):
        def e_rmm(eng, q=q):
            for dc in range(8):
                last = eng.matmul(out=ps_r[:], lhsT=x1_fm[:, dc, q * 128:(q + 1) * 128],
                                  rhs=rw_b[:, dc, :], start=(dc == 0), stop=(dc == 7))
            return last
        p.op("tensor", e_rmm, reads=[x1_fm[:], rw_b[:]], writes=[ps_r[:]])

        def e_logit(eng, q=q):
            return eng.tensor_add(out=gates[:, q, :], in0=ps_r[:], in1=rb_b[:])
        p.op("vector", e_logit, reads=[ps_r[:], rb_b[:]], writes=[gates[:]])

        def e_max(eng, q=q):
            return eng.reduce_max(out=rred[:], in_=gates[:, q, :], axis=AX.X)
        p.op("vector", e_max, reads=[gates[:]], writes=[rred[:]])

        def e_neg(eng):
            return eng.activation(out=rneg[:], in_=rred[:], func=AF.Copy, scale=-1.0)
        p.op("scalar", e_neg, reads=[rred[:]], writes=[rneg[:]])

        def e_iseq(eng, q=q):
            return eng.tensor_tensor(out=riseq[:], in0=gates[:, q, :],
                                     in1=rred[:, 0:1].broadcast_to([128, 8]), op=ALU.is_equal)
        p.op("vector", e_iseq, reads=[gates[:], rred[:]], writes=[riseq[:]])

        def e_exp(eng, q=q):
            return eng.activation(out=gates[:, q, :], in_=gates[:, q, :], func=AF.Exp,
                                  bias=rneg[:, 0:1], accum_out=rsum[:])
        p.op("scalar", e_exp, reads=[gates[:], rneg[:]], writes=[gates[:], rsum[:]])

        def e_rec(eng):
            return eng.reciprocal(out=rsum[:], in_=rsum[:])
        p.op("vector", e_rec, reads=[rsum[:]], writes=[rsum[:]])

        def e_gate(eng, q=q):
            return eng.tensor_mul(out=gates[:, q, :], in0=gates[:, q, :], in1=riseq[:])
        p.op("vector", e_gate, reads=[gates[:], riseq[:]], writes=[gates[:]])

        def e_gsc(eng, q=q):
            return eng.scalar_tensor_tensor(out=gates[:, q, :], in0=gates[:, q, :],
                                            scalar=rsum[:, 0:1], in1=gates[:, q, :],
                                            op0=ALU.mult, op1=ALU.bypass)
        p.op("vector", e_gsc, reads=[gates[:], rsum[:]], writes=[gates[:]])

        def e_gt(eng, q=q):
            return eng.transpose(out=ps_gt, in_=gates[:, q, :], identity=identf[:])
        p.op("tensor", e_gt, reads=[gates[:], identf[:]], writes=[psBf[:]])

        def e_gte(eng, q=q):
            return eng.tensor_copy(out=gatesT[:, q, :], in_=ps_gt)
        p.op("vector", e_gte, reads=[psBf[:]], writes=[gatesT[:]])

    out_acc = p.sbuf([128, NQ, D], BF16)
    gatesT_b = p.sbuf([8, NQ, 128], BF16)

    def e_gtb(eng):
        return eng.activation(out=gatesT_b[:], in_=gatesT[:], func=AF.Copy)
    p.op("scalar", e_gtb, reads=[gatesT[:]], writes=[gatesT_b[:]])

    ps_b2 = psA
    for q in range(NQ):
        for h in range(2):
            def e_mm(eng, q=q, h=h):
                return eng.matmul(out=ps_b2[:], lhsT=gatesT_b[:, q, :],
                                  rhs=b2_b[:, h * 512:(h + 1) * 512], start=True, stop=True)
            p.op("tensor", e_mm, reads=[gatesT_b[:], b2_b[:]], writes=[ps_b2[:]])

            def e_ev(eng, q=q, h=h):
                return eng.tensor_copy(out=out_acc[:, q, h * 512:(h + 1) * 512], in_=ps_b2[:])
            p.op("vector", e_ev, reads=[ps_b2[:]], writes=[out_acc[:]])

    h_fm = p.sbuf([128, 8, TPC], BF16)
    zeros_t = p.sbuf([128, 1024], F32)

    def e_z(eng):
        return eng.memset(zeros_t[:], 0.0)
    p.op("gpsimd", e_z, writes=[zeros_t[:]])
    w1f = [p.sbuf([128, 8, 128], F32) for _ in range(2)]
    w1b = [p.sbuf([128, 8, 128], BF16) for _ in range(2)]
    w2f = [p.sbuf([128, 512], F32) for _ in range(2)]
    w2q_b = p.sbuf([128, 8, D], BF16)
    ps_h = p.psum([128, 1024])
    ps_y = [p.psum([128, 512]) for _ in range(2)]
    for e in range(8):
        for fq in range(4):
            for fc in range(8):
                r0 = fq * 1024 + fc * 128
                for h in range(2):
                    bi = (fc * 2 + h) % 2
                    p.dma("sync", w2f[bi][:], W2[e, r0:r0 + 128, h * 512:(h + 1) * 512])

                    def e_w2c(eng, bi=bi, fc=fc, h=h):
                        return eng.tensor_copy(out=w2q_b[:, fc, h * 512:(h + 1) * 512],
                                               in_=w2f[bi][:])
                    p.op("vector", e_w2c, reads=[w2f[bi][:]], writes=[w2q_b[:]])

            for fc in range(8):
                gfc = fq * 8 + fc
                bi = fc % 2
                for dc in range(8):
                    p.dma("gpsimd", w1f[bi][:, dc, :],
                          W1[e, dc * 128:(dc + 1) * 128, gfc * 128:(gfc + 1) * 128])

                def e_w1c(eng, bi=bi):
                    return eng.activation(out=w1b[bi][:], in_=w1f[bi][:], func=AF.Copy)
                p.op("scalar", e_w1c, reads=[w1f[bi][:]], writes=[w1b[bi][:]])

                for ht in range(2):
                    def e_hmm(eng, bi=bi, ht=ht):
                        for nt in range(2):
                            for dc in range(8):
                                last = eng.matmul(out=ps_h[:, nt * 512:(nt + 1) * 512],
                                                  lhsT=w1b[bi][:, dc, :],
                                                  rhs=x1_fm[:, dc, (ht * 2 + nt) * 512:(ht * 2 + nt + 1) * 512],
                                                  start=(dc == 0), stop=(dc == 7))
                        return last
                    p.op("tensor", e_hmm, reads=[w1b[bi][:], x1_fm[:]], writes=[ps_h[:]])

                    eng_name = "scalar" if (fc + ht) % 2 == 0 else "vector"
                    if eng_name == "scalar":
                        def e_relu(eng, fc=fc, ht=ht, gfc=gfc, e_=e):
                            return eng.activation(out=h_fm[:, fc, ht * 1024:(ht + 1) * 1024],
                                                  in_=ps_h[:],
                                                  func=AF.Relu, bias=b1_s[:, gfc, e_:e_ + 1])
                        p.op(eng_name, e_relu, reads=[ps_h[:], b1_s[:]], writes=[h_fm[:]])
                    else:
                        def e_relu(eng, fc=fc, ht=ht, gfc=gfc, e_=e):
                            return eng.scalar_tensor_tensor(
                                out=h_fm[:, fc, ht * 1024:(ht + 1) * 1024],
                                in0=ps_h[:], scalar=b1_s[:, gfc, e_:e_ + 1],
                                in1=zeros_t[:], op0=ALU.add, op1=ALU.max)
                        p.op(eng_name, e_relu, reads=[ps_h[:], b1_s[:], zeros_t[:]], writes=[h_fm[:]])

            for q in range(NQ):
                for h in range(2):
                    def e_ymm(eng, q=q, h=h):
                        for fc in range(8):
                            last = eng.matmul(out=ps_y[h][:], lhsT=h_fm[:, fc, q * 128:(q + 1) * 128],
                                              rhs=w2q_b[:, fc, h * 512:(h + 1) * 512],
                                              start=(fc == 0), stop=(fc == 7))
                        return last
                    p.op("tensor", e_ymm, reads=[h_fm[:], w2q_b[:]], writes=[ps_y[h][:]])

                    def e_gacc(eng, q=q, h=h, e_=e):
                        return eng.scalar_tensor_tensor(
                            out=out_acc[:, q, h * 512:(h + 1) * 512],
                            in0=ps_y[h][:], scalar=gates[:, q, e_:e_ + 1],
                            in1=out_acc[:, q, h * 512:(h + 1) * 512],
                            op0=ALU.mult, op1=ALU.add)
                    p.op("vector", e_gacc, reads=[ps_y[h][:], gates[:], out_acc[:]],
                         writes=[out_acc[:]])

    xb = [p.sbuf([128, D], F32) for _ in range(2)]
    epst = p.sbuf([128, 1], F32)

    def e_eps(eng):
        return eng.memset(epst[:], 1e-5)
    p.op("gpsimd", e_eps, writes=[epst[:]])
    scratch = (p.sbuf([128, 1], F32), p.sbuf([128, 1], F32), p.sbuf([128, D], F32), epst)
    res = [p.sbuf([128, D], F32) for _ in range(2)]
    for q in range(NQ):
        bi = q % 2
        p.dma("sync", xb[bi][:], x1_tm[q * 128:(q + 1) * 128, :])

        def e_res(eng, q=q, bi=bi):
            return eng.tensor_add(out=res[bi][:], in0=out_acc[:, q, :], in1=xb[bi][:])
        p.op("vector", e_res, reads=[out_acc[:], xb[bi][:]], writes=[res[bi][:]])
        emit_layernorm(p, res[bi], res[bi], None, scratch)
        p.dma("gpsimd", x2[q * 128:(q + 1) * 128, :], res[bi][:])
    return p.build()


def moe(x1_tm, router_W, router_b, W1, b1, W2, b2):
    b1h = np.zeros((128, 32, 8), dtype=np.float32)
    for e in range(8):
        b1h[:, :, e] = b1[e].reshape(32, 128).T
    in_maps = []
    for c in range(NCORES):
        t0 = c * TPC
        in_maps.append({
            "x1_tm": x1_tm[t0:t0 + TPC],
            "rW": router_W, "rb": router_b.reshape(1, 8),
            "W1": W1, "b1h": b1h, "W2": W2, "b2": b2,
        })
    r = run("p4", build_p4, in_maps)
    return np.concatenate([x["x2"] for x in r], axis=0)


def build_p5():
    p = Prog("p5")
    x2_b = p.inp("x2_b", [T, D])
    x2_own = p.inp("x2_own", [TPC, D])
    Wq = p.inp("Wq", [D, D])
    Wk = p.inp("Wk", [D, 512])
    Wv = p.inp("Wv", [D, 512])
    Wo = p.inp("Wo", [D, D])
    ctq = p.inp("ctq", [TPC, 128])
    stq = p.inp("stq", [TPC, 128])
    ctk = p.inp("ctk", [T, 128])
    stk = p.inp("stk", [T, 128])
    masks = p.inp("masks", [16, 128, 512])
    x3 = p.out("x3", [TPC, D])

    ident = add_identity(p)
    identf = add_identity(p, F32)
    NP = 16

    pt = [p.psum([128, 128], BF16) for _ in range(2)]
    ptf = p.psum([128, 128], F32)
    ps_kv = p.psum([128, 1024])
    ps_s = p.psum([128, 512])
    ps_o = p.psum([128, 128])

    wq_b = p.sbuf([128, 8, D], BF16)
    wk_b = p.sbuf([128, 8, 512], BF16)
    wv_b = p.sbuf([128, 8, 512], BF16)
    wld = p.sbuf([128, 512], F32)
    for dc in range(8):
        for hh in range(2):
            p.dma("sync", wld[:], Wq[dc * 128:(dc + 1) * 128, hh * 512:(hh + 1) * 512])

            def e_c(eng, dc=dc, hh=hh):
                return eng.activation(out=wq_b[:, dc, hh * 512:(hh + 1) * 512],
                                      in_=wld[:], func=AF.Copy)
            p.op("scalar", e_c, reads=[wld[:]], writes=[wq_b[:]])
        p.dma("sync", wld[:], Wk[dc * 128:(dc + 1) * 128, :])

        def e_ck(eng, dc=dc):
            return eng.activation(out=wk_b[:, dc, :], in_=wld[:], func=AF.Copy)
        p.op("scalar", e_ck, reads=[wld[:]], writes=[wk_b[:]])
        p.dma("sync", wld[:], Wv[dc * 128:(dc + 1) * 128, :])

        def e_cv2(eng, dc=dc):
            return eng.activation(out=wv_b[:, dc, :], in_=wld[:], func=AF.Copy)
        p.op("scalar", e_cv2, reads=[wld[:]], writes=[wv_b[:]])

    ld = [p.sbuf([128, D], F32) for _ in range(2)]
    cvb = [p.sbuf([128, D], BF16) for _ in range(2)]
    xT_t = p.sbuf([128, 8, 128], BF16)
    kfT = p.sbuf([128, 4, T], BF16)
    qfT = p.sbuf([128, 8, TPC], BF16)
    v_sb = p.sbuf([128, 32, 512], BF16)
    ct_t = p.sbuf([128, 128], F32)
    st_t = p.sbuf([128, 128], F32)
    rot = p.sbuf([128, 1024], F32)
    rf = p.sbuf([128, 1024], F32)
    rfb = p.sbuf([128, 1024], BF16)

    def tile_transpose_in(i, src_hbm):
        bi = i % 2
        p.dma("sync", ld[bi][:], src_hbm)

        def e_cvt(eng, bi=bi):
            return eng.activation(out=cvb[bi][:], in_=ld[bi][:], func=AF.Copy)
        p.op("scalar", e_cvt, reads=[ld[bi][:]], writes=[cvb[bi][:]])
        for j in range(8):
            pj = pt[j % 2]

            def e_tr(eng, bi=bi, j=j, pj=pj):
                return eng.transpose(out=pj[:], in_=cvb[bi][:, j * 128:(j + 1) * 128],
                                     identity=ident[:])
            p.op("tensor", e_tr, reads=[cvb[bi][:], ident[:]], writes=[pj[:]])

            def e_ev(eng, j=j, pj=pj):
                return eng.tensor_copy(out=xT_t[:, j, :], in_=pj[:])
            p.op("vector", e_ev, reads=[pj[:]], writes=[xT_t[:]])

    def rope_apply(ps_view, nheads, dst_fT, dst_col, ct, st):
        n = nheads * 128

        def e_rot1(eng):
            return eng.tensor_copy(
                out=rot[:, 0:n].rearrange("p (h k) -> p h k", k=128)[:, :, 0:64],
                in_=ps_view.rearrange("p (h k) -> p h k", k=128)[:, :, 64:128])
        p.op("vector", e_rot1, reads=[ps_view], writes=[rot[:]])

        def e_rot2(eng):
            return eng.tensor_copy(
                out=rot[:, 0:n].rearrange("p (h k) -> p h k", k=128)[:, :, 64:128],
                in_=ps_view.rearrange("p (h k) -> p h k", k=128)[:, :, 0:64])
        p.op("vector", e_rot2, reads=[ps_view], writes=[rot[:]])

        def e_m1(eng):
            return eng.tensor_tensor(
                out=rf[:, 0:n].rearrange("p (h k) -> p h k", k=128),
                in0=ps_view.rearrange("p (h k) -> p h k", k=128),
                in1=ct[:].unsqueeze(1).broadcast_to([128, nheads, 128]), op=ALU.mult)
        p.op("vector", e_m1, reads=[ps_view, ct[:]], writes=[rf[:]])

        def e_m2(eng):
            return eng.tensor_tensor(
                out=rot[:, 0:n].rearrange("p (h k) -> p h k", k=128),
                in0=rot[:, 0:n].rearrange("p (h k) -> p h k", k=128),
                in1=st[:].unsqueeze(1).broadcast_to([128, nheads, 128]), op=ALU.mult)
        p.op("vector", e_m2, reads=[rot[:], st[:]], writes=[rot[:]])

        def e_add(eng):
            return eng.tensor_add(out=rf[:, 0:n], in0=rf[:, 0:n], in1=rot[:, 0:n])
        p.op("vector", e_add, reads=[rf[:], rot[:]], writes=[rf[:]])

        def e_bf(eng):
            return eng.activation(out=rfb[:, 0:n], in_=rf[:, 0:n], func=AF.Copy)
        p.op("scalar", e_bf, reads=[rf[:]], writes=[rfb[:]])
        for h in range(nheads):
            pj = pt[h % 2]

            def e_tr(eng, h=h, pj=pj):
                return eng.transpose(out=pj[:], in_=rfb[:, h * 128:(h + 1) * 128],
                                     identity=ident[:])
            p.op("tensor", e_tr, reads=[rfb[:], ident[:]], writes=[pj[:]])

            def e_ev(eng, h=h, pj=pj, dst_fT=dst_fT, dst_col=dst_col):
                return eng.tensor_copy(out=dst_fT[:, h, dst_col:dst_col + 128], in_=pj[:])
            p.op("vector", e_ev, reads=[pj[:]], writes=[dst_fT[:]])

    for i in range(32):
        tile_transpose_in(i, x2_b[i * 128:(i + 1) * 128, :])
        p.dma("gpsimd", ct_t[:], ctk[i * 128:(i + 1) * 128, :])
        p.dma("gpsimd", st_t[:], stk[i * 128:(i + 1) * 128, :])

        def e_kv(eng, i=i):
            for dc in range(8):
                eng.matmul(out=ps_kv[:, 0:512], lhsT=xT_t[:, dc, :], rhs=wk_b[:, dc, :],
                           start=(dc == 0), stop=(dc == 7))
            for dc in range(8):
                last = eng.matmul(out=ps_kv[:, 512:1024], lhsT=xT_t[:, dc, :], rhs=wv_b[:, dc, :],
                                  start=(dc == 0), stop=(dc == 7))
            return last
        p.op("tensor", e_kv, reads=[xT_t[:], wk_b[:], wv_b[:]], writes=[ps_kv[:]])

        def e_vev(eng, i=i):
            return eng.activation(out=v_sb[:, i, :], in_=ps_kv[:, 512:1024], func=AF.Copy)
        p.op("scalar", e_vev, reads=[ps_kv[:]], writes=[v_sb[:]])
        rope_apply(ps_kv[:, 0:512], 4, kfT, i * 128, ct_t, st_t)

    for pi in range(NP):
        tile_transpose_in(pi, x2_own[pi * 128:(pi + 1) * 128, :])
        p.dma("gpsimd", ct_t[:], ctq[pi * 128:(pi + 1) * 128, :])
        p.dma("gpsimd", st_t[:], stq[pi * 128:(pi + 1) * 128, :])

        def e_q(eng, pi=pi):
            for hh in range(2):
                for dc in range(8):
                    last = eng.matmul(out=ps_kv[:, hh * 512:(hh + 1) * 512],
                                      lhsT=xT_t[:, dc, :], rhs=wq_b[:, dc, hh * 512:(hh + 1) * 512],
                                      start=(dc == 0), stop=(dc == 7))
            return last
        p.op("tensor", e_q, reads=[xT_t[:], wq_b[:]], writes=[ps_kv[:]])
        rope_apply(ps_kv[:, 0:1024], 8, qfT, pi * 128, ct_t, st_t)

    wo_ld = p.sbuf([128, 512], F32)
    wo_t = p.sbuf([128, 8, D], BF16)
    for dc in range(8):
        for hh in range(2):
            p.dma("sync", wo_ld[:], Wo[dc * 128:(dc + 1) * 128, hh * 512:(hh + 1) * 512])

            def e_woc(eng, dc=dc, hh=hh):
                return eng.activation(out=wo_t[:, dc, hh * 512:(hh + 1) * 512],
                                      in_=wo_ld[:], func=AF.Copy)
            p.op("scalar", e_woc, reads=[wo_ld[:]], writes=[wo_t[:]])

    scor = p.sbuf([128, T], F32)
    mrow = p.sbuf([128, 512], F32)
    rmax = p.sbuf([128, 1], F32)
    rneg = p.sbuf([128, 1], F32)
    rsum = p.sbuf([128, 1], F32)
    ptsb = p.sbuf([128, 4, 128], BF16)
    aot = p.sbuf([128, 8, 128], BF16)
    xb = p.sbuf([128, D], F32)
    resid = p.sbuf([128, D], F32)
    epst = p.sbuf([128, 1], F32)

    def e_eps(eng):
        return eng.memset(epst[:], 1e-5)
    p.op("gpsimd", e_eps, writes=[epst[:]])
    scratch = (p.sbuf([128, 1], F32), p.sbuf([128, 1], F32), p.sbuf([128, D], F32), epst)

    SCALE = 1.0 / math.sqrt(128.0)
    for pi in range(NP):
        nb = pi // 2 + 1
        p.dma("gpsimd", mrow[:], masks[pi, :, :])
        for h in range(8):
            for sb in range(nb):
                def e_smm(eng, pi=pi, h=h, sb=sb):
                    return eng.matmul(out=ps_s[:], lhsT=qfT[:, h, pi * 128:(pi + 1) * 128],
                                      rhs=kfT[:, h // 2, sb * 512:(sb + 1) * 512],
                                      start=True, stop=True)
                p.op("tensor", e_smm, reads=[qfT[:], kfT[:]], writes=[ps_s[:]])
                if sb == nb - 1:
                    def e_cp(eng, sb=sb):
                        return eng.tensor_add(out=scor[:, sb * 512:(sb + 1) * 512],
                                              in0=ps_s[:], in1=mrow[:])
                    p.op("vector", e_cp, reads=[ps_s[:], mrow[:]], writes=[scor[:]])
                else:
                    def e_cp(eng, sb=sb):
                        return eng.tensor_copy(out=scor[:, sb * 512:(sb + 1) * 512], in_=ps_s[:])
                    p.op("vector", e_cp, reads=[ps_s[:]], writes=[scor[:]])
            L = nb * 512

            def e_mx(eng, L=L):
                return eng.reduce_max(out=rmax[:], in_=scor[:, 0:L], axis=AX.X)
            p.op("vector", e_mx, reads=[scor[:]], writes=[rmax[:]])

            def e_ng(eng):
                return eng.activation(out=rneg[:], in_=rmax[:], func=AF.Copy, scale=-SCALE)
            p.op("scalar", e_ng, reads=[rmax[:]], writes=[rneg[:]])

            def e_exp(eng, L=L):
                return eng.activation(out=scor[:, 0:L], in_=scor[:, 0:L], func=AF.Exp,
                                      bias=rneg[:, 0:1], scale=SCALE, accum_out=rsum[:])
            p.op("scalar", e_exp, reads=[scor[:], rneg[:]], writes=[scor[:], rsum[:]])

            def e_rc(eng):
                return eng.reciprocal(out=rsum[:], in_=rsum[:])
            p.op("vector", e_rc, reads=[rsum[:]], writes=[rsum[:]])

            def e_sc(eng, L=L):
                return eng.scalar_tensor_tensor(out=scor[:, 0:L], in0=scor[:, 0:L],
                                                scalar=rsum[:, 0:1], in1=scor[:, 0:L],
                                                op0=ALU.mult, op1=ALU.bypass)
            p.op("vector", e_sc, reads=[scor[:], rsum[:]], writes=[scor[:]])

            for sb in range(nb):
                for j in range(4):
                    def e_pt(eng, sb=sb, j=j):
                        return eng.transpose(out=ptf[:],
                                             in_=scor[:, (sb * 4 + j) * 128:(sb * 4 + j + 1) * 128],
                                             identity=identf[:])
                    p.op("tensor", e_pt, reads=[scor[:], identf[:]], writes=[ptf[:]])

                    def e_pte(eng, j=j):
                        return eng.tensor_copy(out=ptsb[:, j, :], in_=ptf[:])
                    p.op("vector", e_pte, reads=[ptf[:]], writes=[ptsb[:]])

                def e_pv(eng, pi=pi, h=h, sb=sb):
                    for j in range(4):
                        last = eng.matmul(out=ps_o[:], lhsT=v_sb[:, sb * 4 + j, (h // 2) * 128:(h // 2 + 1) * 128],
                                          rhs=ptsb[:, j, :],
                                          start=(sb == 0 and j == 0), stop=(sb == nb - 1 and j == 3))
                    return last
                p.op("tensor", e_pv, reads=[v_sb[:], ptsb[:]], writes=[ps_o[:]])

            def e_oev(eng, h=h):
                return eng.activation(out=aot[:, h, :], in_=ps_o[:], func=AF.Copy)
            p.op("scalar", e_oev, reads=[ps_o[:]], writes=[aot[:]])

        p.dma("sync", xb[:], x2_own[pi * 128:(pi + 1) * 128, :])
        for hh in range(2):
            def e_omm(eng, hh=hh):
                for dc in range(8):
                    last = eng.matmul(out=ps_kv[:, hh * 512:(hh + 1) * 512],
                                      lhsT=aot[:, dc, :], rhs=wo_t[:, dc, hh * 512:(hh + 1) * 512],
                                      start=(dc == 0), stop=(dc == 7))
                return last
            p.op("tensor", e_omm, reads=[aot[:], wo_t[:]], writes=[ps_kv[:]])

            def e_res(eng, hh=hh):
                return eng.tensor_add(out=resid[:, hh * 512:(hh + 1) * 512],
                                      in0=ps_kv[:, hh * 512:(hh + 1) * 512],
                                      in1=xb[:, hh * 512:(hh + 1) * 512])
            p.op("vector", e_res, reads=[ps_kv[:], xb[:]], writes=[resid[:]])
        emit_layernorm(p, resid, resid, None, scratch)
        p.dma("gpsimd", x3[pi * 128:(pi + 1) * 128, :], resid[:])
    return p.build()


def l1_attn(x2_tm, Wq, Wk, Wv, Wo):
    NH_, NKV, Dh = 8, 4, 128
    invf = 1.0 / (10000.0 ** (np.arange(0, Dh, 2) / Dh))
    t_ = np.arange(T)
    fr = np.concatenate([t_[:, None] * invf[None, :]] * 2, axis=1)
    cosf = np.cos(fr).astype(np.float32)
    sinf = np.sin(fr).astype(np.float32)
    sadj = sinf.copy()
    sadj[:, :64] *= -1.0
    in_maps = []
    own_rows_all = []
    for c in range(NCORES):
        b, half = c // 2, c % 2
        own = np.concatenate([np.arange((2 * pp + half) * 128, (2 * pp + half) * 128 + 128)
                              for pp in range(16)])
        own_rows_all.append(b * T + own)
        masks = np.zeros((16, 128, 512), dtype=np.float32)
        for pp in range(16):
            qt = 2 * pp + half
            nb = pp // 2 + 1
            off = qt * 128 - (nb - 1) * 512
            i_ = np.arange(128)[:, None]
            j_ = np.arange(512)[None, :]
            masks[pp] = np.where(j_ > off + i_, -3.0e38, 0.0)
        in_maps.append({
            "x2_b": x2_tm[b * T:(b + 1) * T],
            "x2_own": x2_tm[b * T + own],
            "Wq": Wq, "Wk": Wk, "Wv": Wv, "Wo": Wo,
            "ctq": cosf[own], "stq": sadj[own], "ctk": cosf, "stk": sadj,
            "masks": masks,
        })
    r = run("p5", build_p5, in_maps)
    x3 = np.zeros((NTOK, D), dtype=np.float32)
    for c in range(NCORES):
        x3[own_rows_all[c]] = r[c]["x3"]
    return x3


def build_p6():
    p = Prog("p6")
    NK = 2304
    xh = p.inp("xh", [NK + 1, D])
    Wqkv = p.inp("Wqkv", [D, 3072])
    Wo = p.inp("Wo", [D, D])
    qs_row = p.inp("qs_row", [1, 64])
    ks_row = p.inp("ks_row", [1, 64])
    csq = p.inp("csq", [TPC, 64])
    ssq = p.inp("ssq", [TPC, 64])
    csk = p.inp("csk", [NK, 64])
    ssk = p.inp("ssk", [NK, 64])
    masks = p.inp("masks", [16, 128, 512])
    xo = p.out("xo", [TPC, D])

    ident = add_identity(p)
    identf = add_identity(p, F32)
    ones_col = add_ones_col(p)

    pt = [p.psum([128, 128], BF16) for _ in range(2)]
    ptf = p.psum([128, 128], F32)
    ps_p = p.psum([128, 1024])
    ps_s = p.psum([128, 512])
    ps_o = p.psum([64, 128])

    wqkv_b = p.sbuf([128, 8, 3072], BF16)
    wld = p.sbuf([128, 512], F32)
    for dc in range(8):
        for g in range(6):
            p.dma("sync", wld[:], Wqkv[dc * 128:(dc + 1) * 128, g * 512:(g + 1) * 512])

            def e_c(eng, dc=dc, g=g):
                return eng.activation(out=wqkv_b[:, dc, g * 512:(g + 1) * 512],
                                      in_=wld[:], func=AF.Copy)
            p.op("scalar", e_c, reads=[wld[:]], writes=[wqkv_b[:]])

    qs_b = p.sbuf([128, 64], F32)
    ks_b = p.sbuf([128, 64], F32)
    emit_row_bcast(p, qs_row, qs_b, ones_col, 64, ps=[ps_s])
    emit_row_bcast(p, ks_row, ks_b, ones_col, 64, ps=[ps_s])

    ld = [p.sbuf([128, D], F32)]
    cvb = [p.sbuf([128, D], BF16)]
    xT_t = p.sbuf([128, 8, 128], BF16)
    qf_t = p.sbuf([128, 8, 128], BF16)
    kfT = p.sbuf([128, 8, NK], BF16)
    v_sb = p.sbuf([128, 18, 1024], BF16)
    ct_t = p.sbuf([128, 64], F32)
    st_t = p.sbuf([128, 64], F32)
    gn = p.sbuf([128, 16], F32)
    qn = p.sbuf([128, 1024], F32)
    rot = p.sbuf([128, 1024], F32)
    rfb = p.sbuf([128, 1024], BF16)

    def load_xs_tile(bi, r):
        bi = 0
        """shifted tile r: rows xh[r*128+1 .. +129) cols 0:512, xh[r*128 ..] cols 512:1024"""
        p.dma("sync", ld[bi][:, 0:512], xh[r * 128 + 1:r * 128 + 129, 0:512])
        p.dma("sync", ld[bi][:, 512:1024], xh[r * 128:r * 128 + 128, 512:1024])

        def e_cvt(eng, bi=bi):
            return eng.activation(out=cvb[bi][:], in_=ld[bi][:], func=AF.Copy)
        p.op("scalar", e_cvt, reads=[ld[bi][:]], writes=[cvb[bi][:]])
        for j in range(8):
            pj = pt[j % 2]

            def e_tr(eng, bi=bi, j=j, pj=pj):
                return eng.transpose(out=pj[:], in_=cvb[bi][:, j * 128:(j + 1) * 128],
                                     identity=ident[:])
            p.op("tensor", e_tr, reads=[cvb[bi][:], ident[:]], writes=[pj[:]])

            def e_ev(eng, j=j, pj=pj):
                return eng.tensor_copy(out=xT_t[:, j, :], in_=pj[:])
            p.op("vector", e_ev, reads=[pj[:]], writes=[xT_t[:]])

    def norm_rope(ps_view, sc_b, ct, st, dst_fT, dst_col):
        def e_sq(eng):
            return eng.activation(out=rot[:], in_=ps_view, func=AF.Square)
        p.op("scalar", e_sq, reads=[ps_view], writes=[rot[:]])

        def e_gn(eng):
            return eng.reduce_sum(out=gn[:], in_=rot[:].rearrange("p (g k) -> p g k", k=64),
                                  axis=AX.X)
        p.op("vector", e_gn, reads=[rot[:]], writes=[gn[:]])

        def e_sqrt(eng):
            return eng.activation(out=gn[:], in_=gn[:], func=AF.Sqrt)
        p.op("scalar", e_sqrt, reads=[gn[:]], writes=[gn[:]])

        def e_clip(eng):
            return eng.tensor_scalar_max(out=gn[:], in0=gn[:], scalar1=1e-12)
        p.op("vector", e_clip, reads=[gn[:]], writes=[gn[:]])

        def e_rc(eng):
            return eng.reciprocal(out=gn[:], in_=gn[:])
        p.op("vector", e_rc, reads=[gn[:]], writes=[gn[:]])

        def e_n(eng):
            return eng.tensor_tensor(out=qn[:].rearrange("p (g k) -> p g k", k=64),
                                     in0=ps_view.rearrange("p (g k) -> p g k", k=64),
                                     in1=gn[:].unsqueeze(2).broadcast_to([128, 16, 64]),
                                     op=ALU.mult)
        p.op("vector", e_n, reads=[ps_view, gn[:]], writes=[qn[:]])

        def e_s(eng, sc_b=sc_b):
            return eng.tensor_tensor(out=qn[:].rearrange("p (g k) -> p g k", k=64),
                                     in0=qn[:].rearrange("p (g k) -> p g k", k=64),
                                     in1=sc_b[:].unsqueeze(1).broadcast_to([128, 16, 64]),
                                     op=ALU.mult)
        p.op("vector", e_s, reads=[qn[:], sc_b[:]], writes=[qn[:]])

        def e_r1(eng):
            return eng.tensor_copy(out=rot[:].rearrange("p (g k) -> p g k", k=64)[:, :, 0:32],
                                   in_=qn[:].rearrange("p (g k) -> p g k", k=64)[:, :, 32:64])
        p.op("vector", e_r1, reads=[qn[:]], writes=[rot[:]])

        def e_r2(eng):
            return eng.tensor_copy(out=rot[:].rearrange("p (g k) -> p g k", k=64)[:, :, 32:64],
                                   in_=qn[:].rearrange("p (g k) -> p g k", k=64)[:, :, 0:32])
        p.op("vector", e_r2, reads=[qn[:]], writes=[rot[:]])

        def e_m1(eng, ct=ct):
            return eng.tensor_tensor(out=qn[:].rearrange("p (g k) -> p g k", k=64),
                                     in0=qn[:].rearrange("p (g k) -> p g k", k=64),
                                     in1=ct[:].unsqueeze(1).broadcast_to([128, 16, 64]),
                                     op=ALU.mult)
        p.op("vector", e_m1, reads=[qn[:], ct[:]], writes=[qn[:]])

        def e_m2(eng, st=st):
            return eng.tensor_tensor(out=rot[:].rearrange("p (g k) -> p g k", k=64),
                                     in0=rot[:].rearrange("p (g k) -> p g k", k=64),
                                     in1=st[:].unsqueeze(1).broadcast_to([128, 16, 64]),
                                     op=ALU.mult)
        p.op("vector", e_m2, reads=[rot[:], st[:]], writes=[rot[:]])

        def e_add(eng):
            return eng.tensor_add(out=qn[:], in0=qn[:], in1=rot[:])
        p.op("vector", e_add, reads=[qn[:], rot[:]], writes=[qn[:]])

        def e_bf(eng):
            return eng.activation(out=rfb[:], in_=qn[:], func=AF.Copy)
        p.op("scalar", e_bf, reads=[qn[:]], writes=[rfb[:]])
        for j in range(8):
            pj = pt[j % 2]

            def e_tr(eng, j=j, pj=pj):
                return eng.transpose(out=pj[:], in_=rfb[:, j * 128:(j + 1) * 128],
                                     identity=ident[:])
            p.op("tensor", e_tr, reads=[rfb[:], ident[:]], writes=[pj[:]])

            def e_ev(eng, j=j, pj=pj, dst_fT=dst_fT, dst_col=dst_col):
                return eng.tensor_copy(out=dst_fT[:, j, dst_col:dst_col + 128], in_=pj[:])
            p.op("vector", e_ev, reads=[pj[:]], writes=[dst_fT[:]])

    for r in range(18):
        bi = 0
        load_xs_tile(bi, r)
        p.dma("gpsimd", ct_t[:, 0:64], csk[r * 128:(r + 1) * 128, :])
        p.dma("gpsimd", st_t[:, 0:64], ssk[r * 128:(r + 1) * 128, :])

        def e_kv(eng, r=r):
            for g in (2, 3):
                for dc in range(8):
                    last = eng.matmul(out=ps_p[:, (g - 2) * 512:(g - 1) * 512], lhsT=xT_t[:, dc, :],
                                      rhs=wqkv_b[:, dc, g * 512:(g + 1) * 512],
                                      start=(dc == 0), stop=(dc == 7))
            return last
        p.op("tensor", e_kv, reads=[xT_t[:], wqkv_b[:]], writes=[ps_p[:]])
        norm_rope(ps_p[:, 0:1024], ks_b, ct_t, st_t, kfT, r * 128)

        def e_v(eng, r=r):
            for g in (4, 5):
                for dc in range(8):
                    last = eng.matmul(out=ps_p[:, (g - 4) * 512:(g - 3) * 512],
                                      lhsT=xT_t[:, dc, :],
                                      rhs=wqkv_b[:, dc, g * 512:(g + 1) * 512],
                                      start=(dc == 0), stop=(dc == 7))
            return last
        p.op("tensor", e_v, reads=[xT_t[:], wqkv_b[:]], writes=[ps_p[:]])

        def e_vev(eng, r=r):
            return eng.activation(out=v_sb[:, r, :], in_=ps_p[:, 0:1024], func=AF.Copy)
        p.op("scalar", e_vev, reads=[ps_p[:]], writes=[v_sb[:]])

    wo_b = p.sbuf([128, 8, D], BF16)
    for dc in range(8):
        for hh in range(2):
            p.dma("sync", wld[:], Wo[dc * 128:(dc + 1) * 128, hh * 512:(hh + 1) * 512])

            def e_woc(eng, dc=dc, hh=hh):
                return eng.activation(out=wo_b[:, dc, hh * 512:(hh + 1) * 512],
                                      in_=wld[:], func=AF.Copy)
            p.op("scalar", e_woc, reads=[wld[:]], writes=[wo_b[:]])

    scor = p.sbuf([128, 512], F32)
    mrow = p.sbuf([128, 512], F32)
    rmax = p.sbuf([128, 1], F32)
    rneg = p.sbuf([128, 1], F32)
    rsum = p.sbuf([128, 1], F32)
    ptsb = p.sbuf([128, 4, 128], BF16)
    aot = p.sbuf([128, 8, 128], BF16)
    xb = p.sbuf([128, D], F32)
    resid = p.sbuf([128, D], F32)
    QKS = 8.0
    for qt in range(16):
        w = qt // 2
        load_xs_tile(0, qt + 2)
        p.dma("gpsimd", ct_t[:, 0:64], csq[qt * 128:(qt + 1) * 128, :])
        p.dma("gpsimd", st_t[:, 0:64], ssq[qt * 128:(qt + 1) * 128, :])

        def e_q(eng, qt=qt):
            for g in (0, 1):
                for dc in range(8):
                    last = eng.matmul(out=ps_p[:, g * 512:(g + 1) * 512], lhsT=xT_t[:, dc, :],
                                      rhs=wqkv_b[:, dc, g * 512:(g + 1) * 512],
                                      start=(dc == 0), stop=(dc == 7))
            return last
        p.op("tensor", e_q, reads=[xT_t[:], wqkv_b[:]], writes=[ps_p[:]])
        norm_rope(ps_p[:, 0:1024], qs_b, ct_t, st_t, qf_t, 0)
        p.dma("gpsimd", mrow[:], masks[qt, :, :])
        for h in range(16):
            pb = 64 * (h % 2)
            hc = h // 2

            def e_smm(eng, qt=qt, h=h, w=w, pb=pb, hc=hc):
                return eng.matmul(out=ps_s[:], lhsT=qf_t[pb:pb + 64, hc, :],
                                  rhs=kfT[pb:pb + 64, hc, w * 256:w * 256 + 512],
                                  start=True, stop=True)
            p.op("tensor", e_smm, reads=[qf_t[:], kfT[:]], writes=[ps_s[:]])

            def e_cp(eng):
                return eng.tensor_add(out=scor[:], in0=ps_s[:], in1=mrow[:])
            p.op("vector", e_cp, reads=[ps_s[:], mrow[:]], writes=[scor[:]])

            def e_mx(eng):
                return eng.reduce_max(out=rmax[:], in_=scor[:], axis=AX.X)
            p.op("vector", e_mx, reads=[scor[:]], writes=[rmax[:]])

            def e_ng(eng):
                return eng.activation(out=rneg[:], in_=rmax[:], func=AF.Copy, scale=-QKS)
            p.op("scalar", e_ng, reads=[rmax[:]], writes=[rneg[:]])

            def e_exp(eng):
                return eng.activation(out=scor[:], in_=scor[:], func=AF.Exp,
                                      bias=rneg[:, 0:1], scale=QKS, accum_out=rsum[:])
            p.op("scalar", e_exp, reads=[scor[:], rneg[:]], writes=[scor[:], rsum[:]])

            def e_rc(eng):
                return eng.reciprocal(out=rsum[:], in_=rsum[:])
            p.op("vector", e_rc, reads=[rsum[:]], writes=[rsum[:]])

            def e_sc(eng):
                return eng.scalar_tensor_tensor(out=scor[:], in0=scor[:],
                                                scalar=rsum[:, 0:1], in1=scor[:],
                                                op0=ALU.mult, op1=ALU.bypass)
            p.op("vector", e_sc, reads=[scor[:], rsum[:]], writes=[scor[:]])

            for j in range(4):
                def e_pt(eng, j=j):
                    return eng.transpose(out=ptf[:], in_=scor[:, j * 128:(j + 1) * 128],
                                         identity=identf[:])
                p.op("tensor", e_pt, reads=[scor[:], identf[:]], writes=[ptf[:]])

                def e_pte(eng, j=j):
                    return eng.tensor_copy(out=ptsb[:, j, :], in_=ptf[:])
                p.op("vector", e_pte, reads=[ptf[:]], writes=[ptsb[:]])

            def e_pv(eng, qt=qt, h=h, w=w):
                for j in range(4):
                    last = eng.matmul(out=ps_o[:], lhsT=v_sb[:, w * 2 + j, h * 64:(h + 1) * 64],
                                      rhs=ptsb[:, j, :], start=(j == 0), stop=(j == 3))
                return last
            p.op("tensor", e_pv, reads=[v_sb[:], ptsb[:]], writes=[ps_o[:]])

            def e_oev(eng, h=h, pb=pb, hc=hc):
                return eng.activation(out=aot[pb:pb + 64, hc, :], in_=ps_o[:], func=AF.Copy)
            p.op("scalar", e_oev, reads=[ps_o[:]], writes=[aot[:]])

        p.dma("sync", xb[:], xh[257 + qt * 128:257 + (qt + 1) * 128, :])
        for hh in range(2):
            def e_omm(eng, hh=hh):
                for dc in range(8):
                    last = eng.matmul(out=ps_p[:, hh * 512:(hh + 1) * 512],
                                      lhsT=aot[:, dc, :], rhs=wo_b[:, dc, hh * 512:(hh + 1) * 512],
                                      start=(dc == 0), stop=(dc == 7))
                return last
            p.op("tensor", e_omm, reads=[aot[:], wo_b[:]], writes=[ps_p[:]])

            def e_res(eng, hh=hh):
                return eng.tensor_add(out=resid[:, hh * 512:(hh + 1) * 512],
                                      in0=ps_p[:, hh * 512:(hh + 1) * 512],
                                      in1=xb[:, hh * 512:(hh + 1) * 512])
            p.op("vector", e_res, reads=[ps_p[:], xb[:]], writes=[resid[:]])
        p.dma("gpsimd", xo[qt * 128:(qt + 1) * 128, :], resid[:])
    return p.build()


def build_p7():
    p = Prog("p7")
    FFP = 2816
    CH = [128] * 22
    HALVES = [(0, 11), (11, 11)]
    xh = p.inp("xh", [TPC + 1, D])
    W1a = p.inp("W1a", [D, FFP])
    W1g = p.inp("W1g", [D, FFP])
    b1a = p.inp("b1a", [128, 22])
    b1g = p.inp("b1g", [128, 22])
    W2 = p.inp("W2", [FFP, D])
    b2r = p.inp("b2r", [1, D])
    lng = p.inp("lng", [1, D])
    lnb = p.inp("lnb", [1, D])
    xo = p.out("xo", [TPC, D])

    ident = add_identity(p)
    ones_col = add_ones_col(p)
    NQ = 16

    pt = [p.psum([128, 128], BF16) for _ in range(2)]
    ps_a = p.psum([128, 512])
    ps_g = p.psum([128, 512])
    ps_y = [p.psum([128, 512]) for _ in range(2)]

    lng_b = p.sbuf([128, D], F32)
    lnb_b = p.sbuf([128, D], F32)
    b2_b = p.sbuf([128, D], F32)
    emit_row_bcast(p, lng, lng_b, ones_col, D, ps=ps_y)
    emit_row_bcast(p, lnb, lnb_b, ones_col, D, ps=ps_y)
    emit_row_bcast(p, b2r, b2_b, ones_col, D, ps=ps_y)
    b1a_s = p.sbuf([128, 22], F32)
    b1g_s = p.sbuf([128, 22], F32)
    p.dma("sync", b1a_s[:], b1a)
    p.dma("sync", b1g_s[:], b1g)

    zeros_t = p.sbuf([128, 512], F32)

    def e_z(eng):
        return eng.memset(zeros_t[:], 0.0)
    p.op("gpsimd", e_z, writes=[zeros_t[:]])
    epst = p.sbuf([128, 1], F32)

    def e_eps(eng):
        return eng.memset(epst[:], 1e-5)
    p.op("gpsimd", e_eps, writes=[epst[:]])
    scratch = (p.sbuf([128, 1], F32), p.sbuf([128, 1], F32), p.sbuf([128, D], F32), epst)

    ld = [p.sbuf([128, D], F32) for _ in range(2)]
    cvb = [p.sbuf([128, D], BF16) for _ in range(2)]
    xnT = p.sbuf([128, 8, TPC], BF16)
    for r in range(NQ):
        bi = r % 2
        p.dma("sync", ld[bi][:, 0:512], xh[r * 128 + 1:r * 128 + 129, 0:512])
        p.dma("sync", ld[bi][:, 512:1024], xh[r * 128:r * 128 + 128, 512:1024])
        emit_layernorm(p, ld[bi], ld[bi], (lng_b, lnb_b), scratch)

        def e_cvt(eng, bi=bi):
            return eng.activation(out=cvb[bi][:], in_=ld[bi][:], func=AF.Copy)
        p.op("scalar", e_cvt, reads=[ld[bi][:]], writes=[cvb[bi][:]])
        for j in range(8):
            pj = pt[j % 2]

            def e_tr(eng, bi=bi, j=j, pj=pj):
                return eng.transpose(out=pj[:], in_=cvb[bi][:, j * 128:(j + 1) * 128],
                                     identity=ident[:])
            p.op("tensor", e_tr, reads=[cvb[bi][:], ident[:]], writes=[pj[:]])

            def e_ev(eng, r=r, j=j, pj=pj):
                return eng.tensor_copy(out=xnT[:, j, r * 128:(r + 1) * 128], in_=pj[:])
            p.op("vector", e_ev, reads=[pj[:]], writes=[xnT[:]])

    w1lf = [p.sbuf([128, 8, 128], F32) for _ in range(2)]
    w1lb = [p.sbuf([128, 8, 128], BF16) for _ in range(2)]
    w2lf = p.sbuf([128, 512], F32)
    mT = p.sbuf([128, 11, TPC], BF16)
    w2h_b = p.sbuf([128, 11, D], BF16)
    ha = p.sbuf([128, 512], F32)
    ge = p.sbuf([128, 512], F32)
    acc = p.sbuf([128, NQ, D], BF16)
    xb = p.sbuf([128, D], F32)
    t0_ = p.sbuf([128, D], F32)
    for hf, (c0, nch) in enumerate(HALVES):
        for fc in range(nch):
            gfc = c0 + fc
            fw = CH[gfc]
            r0 = gfc * 128
            for hh in range(2):
                p.dma("sync", w2lf[0:fw, :], W2[r0:r0 + fw, hh * 512:(hh + 1) * 512])

                def e_w2c(eng, fc=fc, hh=hh, fw=fw):
                    return eng.tensor_copy(out=w2h_b[0:fw, fc, hh * 512:(hh + 1) * 512],
                                           in_=w2lf[0:fw, :])
                p.op("vector", e_w2c, reads=[w2lf[:]], writes=[w2h_b[:]])

        for fc in range(nch):
            gfc = c0 + fc
            fw = CH[gfc]
            f0 = gfc * 128
            bi = fc % 2
            for dc in range(8):
                p.dma("gpsimd", w1lf[bi][:, dc, 0:fw], W1a[dc * 128:(dc + 1) * 128, f0:f0 + fw])

            def e_w1c(eng, bi=bi, fw=fw):
                return eng.activation(out=w1lb[bi][:, :, 0:fw], in_=w1lf[bi][:, :, 0:fw],
                                      func=AF.Copy)
            p.op("scalar", e_w1c, reads=[w1lf[bi][:]], writes=[w1lb[bi][:]])
            for nt in range(4):
                def e_amm(eng, bi=bi, fw=fw, nt=nt):
                    for dc in range(8):
                        last = eng.matmul(out=ps_a[0:fw, :], lhsT=w1lb[bi][:, dc, 0:fw],
                                          rhs=xnT[:, dc, nt * 512:(nt + 1) * 512],
                                          start=(dc == 0), stop=(dc == 7))
                    return last
                p.op("tensor", e_amm, reads=[w1lb[bi][:], xnT[:]], writes=[ps_a[:]])

                def e_ha(eng, fw=fw, gfc=gfc):
                    return eng.scalar_tensor_tensor(out=ha[0:fw, :], in0=ps_a[0:fw, :],
                                                    scalar=b1a_s[0:fw, gfc:gfc + 1],
                                                    in1=zeros_t[0:fw, :],
                                                    op0=ALU.add, op1=ALU.add)
                p.op("vector", e_ha, reads=[ps_a[:], b1a_s[:], zeros_t[:]], writes=[ha[:]])

                bj = (fc + 1) % 2
                if nt == 0:
                    for dc in range(8):
                        p.dma("gpsimd", w1lf[bj][:, dc, 0:fw],
                              W1g[dc * 128:(dc + 1) * 128, f0:f0 + fw])

                    def e_w1gc(eng, bj=bj, fw=fw):
                        return eng.activation(out=w1lb[bj][:, :, 0:fw], in_=w1lf[bj][:, :, 0:fw],
                                              func=AF.Copy)
                    p.op("scalar", e_w1gc, reads=[w1lf[bj][:]], writes=[w1lb[bj][:]])

                def e_gmm(eng, bj=bj, fw=fw, nt=nt):
                    for dc in range(8):
                        last = eng.matmul(out=ps_g[0:fw, :], lhsT=w1lb[bj][:, dc, 0:fw],
                                          rhs=xnT[:, dc, nt * 512:(nt + 1) * 512],
                                          start=(dc == 0), stop=(dc == 7))
                    return last
                p.op("tensor", e_gmm, reads=[w1lb[bj][:], xnT[:]], writes=[ps_g[:]])

                def e_ge(eng, fw=fw, gfc=gfc):
                    return eng.activation(out=ge[0:fw, :], in_=ps_g[0:fw, :], func=AF.Gelu,
                                          bias=b1g_s[0:fw, gfc:gfc + 1])
                p.op("scalar", e_ge, reads=[ps_g[:], b1g_s[:]], writes=[ge[:]])

                def e_m(eng, fw=fw, fc=fc, nt=nt):
                    return eng.tensor_mul(out=mT[0:fw, fc, nt * 512:(nt + 1) * 512],
                                          in0=ha[0:fw, :], in1=ge[0:fw, :])
                p.op("vector", e_m, reads=[ha[:], ge[:]], writes=[mT[:]])

        for q in range(NQ):
            if hf == 1:
                p.dma("sync", xb[:], xh[1 + q * 128:1 + (q + 1) * 128, :])
            for hh in range(2):
                def e_ymm(eng, q=q, hh=hh, c0=c0, nch=nch):
                    for fc in range(nch):
                        fw = CH[c0 + fc]
                        last = eng.matmul(out=ps_y[hh][:], lhsT=mT[0:fw, fc, q * 128:(q + 1) * 128],
                                          rhs=w2h_b[0:fw, fc, hh * 512:(hh + 1) * 512],
                                          start=(fc == 0), stop=(fc == nch - 1))
                    return last
                p.op("tensor", e_ymm, reads=[mT[:], w2h_b[:]], writes=[ps_y[hh][:]])
                if hf == 0:
                    def e_acc(eng, q=q, hh=hh):
                        return eng.tensor_copy(out=acc[:, q, hh * 512:(hh + 1) * 512],
                                               in_=ps_y[hh][:])
                    p.op("vector", e_acc, reads=[ps_y[hh][:]], writes=[acc[:]])
                else:
                    def e_c1(eng, q=q, hh=hh):
                        return eng.tensor_add(out=t0_[:, hh * 512:(hh + 1) * 512],
                                              in0=ps_y[hh][:], in1=acc[:, q, hh * 512:(hh + 1) * 512])
                    p.op("vector", e_c1, reads=[ps_y[hh][:], acc[:]], writes=[t0_[:]])

                    def e_c2(eng, q=q, hh=hh):
                        return eng.tensor_add(out=t0_[:, hh * 512:(hh + 1) * 512],
                                              in0=t0_[:, hh * 512:(hh + 1) * 512],
                                              in1=b2_b[:, hh * 512:(hh + 1) * 512])
                    p.op("vector", e_c2, reads=[t0_[:], b2_b[:]], writes=[t0_[:]])

                    def e_c3(eng, q=q, hh=hh):
                        return eng.tensor_add(out=t0_[:, hh * 512:(hh + 1) * 512],
                                              in0=t0_[:, hh * 512:(hh + 1) * 512],
                                              in1=xb[:, hh * 512:(hh + 1) * 512])
                    p.op("vector", e_c3, reads=[t0_[:], xb[:]], writes=[t0_[:]])
            if hf == 1:
                p.dma("gpsimd", xo[q * 128:(q + 1) * 128, :], t0_[:])
    return p.build()


def build_p8():
    p = Prog("p8")
    a_in = p.inp("a_in", [TPC, D])
    b_in = p.inp("b_in", [TPC, D])
    o = p.out("o", [TPC, D])
    epst = p.sbuf([128, 1], F32)

    def e_eps(eng):
        return eng.memset(epst[:], 1e-5)
    p.op("gpsimd", e_eps, writes=[epst[:]])
    scratch = (p.sbuf([128, 1], F32), p.sbuf([128, 1], F32), p.sbuf([128, D], F32), epst)
    sa = [p.sbuf([128, D], F32) for _ in range(2)]
    sb = [p.sbuf([128, D], F32) for _ in range(2)]
    for q in range(TPC // 128):
        bi = q % 2
        p.dma("sync", sa[bi][:], a_in[q * 128:(q + 1) * 128, :])
        p.dma("sync", sb[bi][:], b_in[q * 128:(q + 1) * 128, :])

        def e_add(eng, bi=bi):
            return eng.tensor_add(out=sa[bi][:], in0=sa[bi][:], in1=sb[bi][:])
        p.op("vector", e_add, reads=[sa[bi][:], sb[bi][:]], writes=[sa[bi][:]])
        emit_layernorm(p, sa[bi], sa[bi], None, scratch)
        p.dma("gpsimd", o[q * 128:(q + 1) * 128, :], sa[bi][:])
    return p.build()


def _xpos_tables():
    DIM_HEAD, WINDOW = 64, 256
    half = np.arange(0, DIM_HEAD, 2, dtype=np.float64)
    inv_freq = 1.0 / 10000.0 ** (half / DIM_HEAD)
    t_ = np.arange(T, dtype=np.float64)
    fr = t_[:, None] * inv_freq[None, :]
    fr = np.concatenate([fr, fr], axis=-1)
    base = (half + 0.4 * DIM_HEAD) / (1.4 * DIM_HEAD)
    power = (t_ - T // 2) / (WINDOW // 2)
    sc = base[None, :] ** power[:, None]
    sc = np.concatenate([sc, sc], axis=-1)
    cos, sin = np.cos(fr), np.sin(fr)
    sadj = sin.copy()
    sadj[:, 0:32] *= -1.0
    csq = (cos * sc).astype(np.float32)
    ssq = (sadj * sc).astype(np.float32)
    csk = (cos / sc).astype(np.float32)
    ssk = (sadj / sc).astype(np.float32)
    return csq, ssq, csk, ssk


def _shift_pad_rows(x_tm, lo, hi, b):
    n = hi - lo
    out = np.zeros((n, D), dtype=np.float32)
    s0 = max(lo, b * T)
    s1 = min(hi, (b + 1) * T)
    if s1 > s0:
        out[s0 - lo:s1 - lo] = x_tm[s0:s1]
    return out


def l1_ffn_layer_attn(x_tm, i, Wqkv, q_scale, k_scale, Wo):
    csq, ssq, csk, ssk = _XPOS
    in_maps = []
    for c in range(NCORES):
        b, half = c // 2, c % 2
        t0 = c * TPC
        tb = t0 - b * T
        masks = np.zeros((16, 128, 512), dtype=np.float32)
        i_ = np.arange(128)[:, None]
        j_ = np.arange(512)[None, :]
        for qt in range(16):
            qoff = (qt % 2) * 128
            m = np.where((j_ < 256) & (j_ < qoff + i_), -3.0e38, 0.0)
            m = np.where((j_ >= 256) & (j_ - 256 > qoff + i_), -3.0e38, m)
            if half == 0 and qt // 2 == 0:
                m[:, 0:256] = -3.0e38
            masks[qt] = m
        in_maps.append({
            "xh": _shift_pad_rows(x_tm, t0 - 257, t0 + TPC, b),
            "Wqkv": np.ascontiguousarray(Wqkv[i]),
            "Wo": np.ascontiguousarray(Wo[i]),
            "qs_row": np.ascontiguousarray(q_scale[i]).reshape(1, 64),
            "ks_row": np.ascontiguousarray(k_scale[i]).reshape(1, 64),
            "csq": csq[tb:tb + TPC], "ssq": ssq[tb:tb + TPC],
            "csk": np.ascontiguousarray(_shift_pad_tab(csq, tb - 256, tb + TPC)),
            "ssk": np.ascontiguousarray(_shift_pad_tab(ssq, tb - 256, tb + TPC)) if False else
                   np.ascontiguousarray(_shift_pad_tab(ssk, tb - 256, tb + TPC)),
            "masks": masks,
        })
        in_maps[-1]["csk"] = np.ascontiguousarray(_shift_pad_tab(csk, tb - 256, tb + TPC))
    r = run("p6", build_p6, in_maps)
    out = x_tm.copy()
    for c in range(NCORES):
        t0 = c * TPC
        out[t0:t0 + TPC] = r[c]["xo"]
    return out


def _shift_pad_tab(tab, lo, hi):
    n = hi - lo
    out = np.zeros((n, tab.shape[1]), dtype=np.float32)
    s0 = max(lo, 0)
    s1 = min(hi, tab.shape[0])
    if s1 > s0:
        out[s0 - lo:s1 - lo] = tab[s0:s1]
    return out


def l1_ffn_layer_ffn(x_tm, i, ln_g, ln_b, W1, b1, W2, b2):
    FF = 2730
    b1a = np.zeros((128, 22), dtype=np.float32)
    b1g = np.zeros((128, 22), dtype=np.float32)
    pa = np.zeros(22 * 128, dtype=np.float32)
    pa[:FF] = b1[i, :FF]
    b1a[:] = pa.reshape(22, 128).T
    pg = np.zeros(22 * 128, dtype=np.float32)
    pg[:FF] = b1[i, FF:]
    b1g[:] = pg.reshape(22, 128).T
    W1a = np.zeros((D, 2816), dtype=np.float32)
    W1a[:, :FF] = W1[i, :, :FF]
    W1g = np.zeros((D, 2816), dtype=np.float32)
    W1g[:, :FF] = W1[i, :, FF:]
    W2i = np.zeros((2816, D), dtype=np.float32)
    W2i[:FF] = W2[i]
    in_maps = []
    for c in range(NCORES):
        b = c // 2
        t0 = c * TPC
        in_maps.append({
            "xh": _shift_pad_rows(x_tm, t0 - 1, t0 + TPC, b),
            "W1a": W1a, "W1g": W1g, "b1a": b1a, "b1g": b1g,
            "W2": W2i, "b2r": np.ascontiguousarray(b2[i]).reshape(1, D),
            "lng": np.ascontiguousarray(ln_g[i]).reshape(1, D),
            "lnb": np.ascontiguousarray(ln_b[i]).reshape(1, D),
        })
    r = run("p7", build_p7, in_maps)
    out = x_tm.copy()
    for c in range(NCORES):
        t0 = c * TPC
        out[t0:t0 + TPC] = r[c]["xo"]
    return out


_XPOS = _xpos_tables()


def final_ln_sum(a_tm, b_tm):
    in_maps = []
    for c in range(NCORES):
        t0 = c * TPC
        in_maps.append({"a_in": a_tm[t0:t0 + TPC], "b_in": b_tm[t0:t0 + TPC]})
    r = run("p8", build_p8, in_maps)
    return np.concatenate([x["o"] for x in r], axis=0)
